# version 17
# baseline (speedup 1.0000x reference)
import numpy as np
import ml_dtypes
import concourse.bass as bass
import concourse.mybir as mybir
from concourse.bass_utils import run_bass_kernel_spmd

F32 = mybir.dt.float32
BF16 = mybir.dt.bfloat16
I32 = mybir.dt.int32
AX = mybir.AxisListType.X
AF = mybir.ActivationFunctionType
ALU = mybir.AluOpType

B_, N_, C_ = 4, 4096, 1024
NCORES = 8
T = (B_ * N_) // NCORES
C8 = 8
H24 = 24
HD = 16
FS = 512
AS = 256
SHIFT = 4
S127 = float(np.float32(1.0 / 127.0))
DOWN_SCALE = float(np.float32(0.5) * np.float32(1.0 / 127.0) * np.float32(1.0 / 127.0))
EPS = 1e-5
NEG = -1e9

STAGE = 3


class Builder:
    ENGINES = ("tensor", "vector", "scalar", "sync", "gpsimd")

    def __init__(self):
        self.ops = {e: [] for e in self.ENGINES}
        self.cnt = {e: 0 for e in self.ENGINES}
        self.chan = {}

    def emit(self, eng, fn, waits=(), inc=False, ch=None):
        ws = []
        for w in waits:
            if w is None:
                continue
            if isinstance(w, list):
                ws.extend(x for x in w if x is not None)
            else:
                ws.append(w)
        if ch is not None:
            prev = self.chan.get(ch, 0)
            if prev:
                ws.append((ch, prev))
            self.chan[ch] = prev + 16
            self.ops[eng].append((ws, fn, False, ch))
            return (ch, prev + 16)
        self.ops[eng].append((ws, fn, inc, None))
        if inc:
            self.cnt[eng] += 1
            return (eng, self.cnt[eng])
        return None

    def t(self, fn, waits=(), inc=True):
        return self.emit("tensor", fn, waits, inc=inc)

    def v(self, fn, waits=(), inc=True):
        return self.emit("vector", fn, waits, inc=inc)

    def s(self, fn, waits=(), inc=True):
        return self.emit("scalar", fn, waits, inc=inc)

    def ld(self, fn, waits=(), ch="c0"):
        return self.emit("sync", fn, waits, ch=ch)

    def gld(self, fn, waits=(), ch="g0"):
        return self.emit("gpsimd", fn, waits, ch=ch)

    def replay(self, nc):
        import contextlib
        stack = contextlib.ExitStack()
        sems = {}
        for e in self.ENGINES:
            sems[e] = stack.enter_context(nc.semaphore("s_" + e))
        for ch in self.chan:
            sems[ch] = stack.enter_context(nc.semaphore("sch_" + ch))
        block = stack.enter_context(nc.Block())

        def run_engine(ename):
            def body(eng):
                waited = {}
                for ws, fn, inc, ch in self.ops[ename]:
                    for key, val in ws:
                        if key == ename:
                            continue
                        if waited.get(key, 0) >= val:
                            continue
                        eng.wait_ge(sems[key], val)
                        waited[key] = val
                    inst = fn(eng)
                    if ch is not None:
                        inst.then_inc(sems[ch], 16)
                    elif inc:
                        inst.then_inc(sems[ename], 1)
            return body

        block.tensor(run_engine("tensor"))
        block.vector(run_engine("vector"))
        block.scalar(run_engine("scalar"))
        block.sync(run_engine("sync"))
        block.gpsimd(run_engine("gpsimd"))
        stack.close()


def build_nc(stage=STAGE):
    nc = bass.Bass("TRN2", target_bir_lowering=False, detect_race_conditions=False)
    b = Builder()

    xT_h = nc.dram_tensor("xT", [C_, T], F32, kind="ExternalInput").ap()
    wqkv_h = nc.dram_tensor("wqkv", [C_, 3 * C_], F32, kind="ExternalInput").ap()
    wproj_h = nc.dram_tensor("wproj", [C_, C_], F32, kind="ExternalInput").ap()
    wgT_h = nc.dram_tensor("wgT", [C_, 3 * C_], I32, kind="ExternalInput").ap()
    wuT_h = nc.dram_tensor("wuT", [C_, 3 * C_], I32, kind="ExternalInput").ap()
    wdT_h = nc.dram_tensor("wdT", [3 * C_, C_], I32, kind="ExternalInput").ap()
    bq_h = nc.dram_tensor("bq", [128, H24], F32, kind="ExternalInput").ap()
    bp_h = nc.dram_tensor("bp", [128, C8], F32, kind="ExternalInput").ap()
    bias_h = nc.dram_tensor("biasm", [128, HD, 128], F32, kind="ExternalInput").ap()
    ident_h = nc.dram_tensor("ident", [128, 128], F32, kind="ExternalInput").ap()
    onesc_h = nc.dram_tensor("onesc", [128, 1], F32, kind="ExternalInput").ap()
    epsc_h = nc.dram_tensor("epsc", [128, 1], F32, kind="ExternalInput").ap()
    onesr_h = nc.dram_tensor("onesr", [1, 128], F32, kind="ExternalInput").ap()
    yT_h = nc.dram_tensor("yT", [C_, T], F32, kind="ExternalOutput").ap()
    wgu_bf_h = nc.dram_tensor("wgu_bf", [H24, 128, 2048], BF16).ap()
    wd_bf_h = nc.dram_tensor("wd_bf", [C8, 128, 3072], BF16).ap()
    wq_bf_h = nc.dram_tensor("wq_bf", [H24, 128, 2048], BF16).ap()
    wp_bf_h = nc.dram_tensor("wp_bf", [C8, 128, 2048], BF16).ap()

    x_sb = nc.alloc_sbuf_tensor("x_sb", [128, C8, T], F32).ap()
    xn = [(nc.alloc_sbuf_tensor(f"xnh{i}", [128, C8, FS], BF16).ap(),
           nc.alloc_sbuf_tensor(f"xnl{i}", [128, C8, FS], BF16).ap()) for i in range(2)]
    ha = nc.alloc_sbuf_tensor("ha", [128, H24 * FS], BF16)
    hb = nc.alloc_sbuf_tensor("hb", [128, H24 * FS], BF16)
    ha_ad = nc.lookup_mloc(ha).addr
    hb_ad = nc.lookup_mloc(hb).addr
    hh = nc.alloc_sbuf_tensor_at("hh", [128, H24, FS], BF16, offset=ha_ad).ap()
    hl = nc.alloc_sbuf_tensor_at("hl", [128, H24, FS], BF16, offset=hb_ad).ap()
    qkv_sb = nc.alloc_sbuf_tensor_at("qkv_sb", [128, H24, AS], F32, offset=ha_ad).ap()
    VT = nc.alloc_sbuf_tensor_at("VT", [128, 2, 1024], F32, offset=hb_ad).ap()
    O_sb = nc.alloc_sbuf_tensor_at("O_sb", [128, 2, 1024], F32, offset=hb_ad + 8192).ap()
    oTh = nc.alloc_sbuf_tensor_at("oTh", [128, C8, AS], BF16, offset=hb_ad + 16384).ap()
    oTl = nc.alloc_sbuf_tensor_at("oTl", [128, C8, AS], BF16, offset=hb_ad + 20480).ap()
    wa = nc.alloc_sbuf_tensor("wa", [128, 20992], mybir.dt.uint8)
    wa_ad = nc.lookup_mloc(wa).addr
    wgu = [nc.alloc_sbuf_tensor_at(f"wgu{i}", [128, C8, 2, 128], BF16,
                                   offset=wa_ad + i * 4096).ap() for i in range(2)]
    wqb = wgu
    wpb = wgu
    wpair = nc.alloc_sbuf_tensor_at("wpair", [128, 2, 2048], BF16, offset=wa_ad).ap()
    wdn = [nc.alloc_sbuf_tensor_at(f"wdn{i}", [128, H24, 128], BF16,
                                   offset=wa_ad + 8448 + i * 6144).ap() for i in range(2)]
    wdpair = nc.alloc_sbuf_tensor_at("wdpair", [128, 2, 3072], BF16,
                                     offset=wa_ad + 8448).ap()
    wf32 = [nc.alloc_sbuf_tensor_at(f"wf32{i}", [128, C8, 128], F32,
                                    offset=wa_ad + 8448 + i * 4096).ap() for i in range(2)]
    pw = [nc.alloc_sbuf_tensor_at(f"pw{i}", [128, C8, 2, 128], BF16,
                                  offset=hb_ad + 8192 + i * 4096).ap()
          for i in range(2)]
    wq_x = [nc.alloc_sbuf_tensor_at(f"wq_x{i}", [128, C8, 2, 128], BF16,
                                    offset=wa_ad + 8448 + i * 4096).ap() for i in range(2)]
    wslots = [wgu[0], wgu[1], wq_x[0], wq_x[1]]

    scrq = [nc.alloc_sbuf_tensor(f"scrq{i}", [128, FS], F32).ap() for i in range(2)]
    scr_h = nc.alloc_sbuf_tensor("scr_h", [128, FS], F32).ap()
    sA = [nc.alloc_sbuf_tensor(f"sA{i}", [128, FS], F32).ap() for i in range(2)]
    mb = nc.alloc_sbuf_tensor("mb", [128, FS], F32).ap()
    rb = nc.alloc_sbuf_tensor("rb", [128, FS], F32).ap()
    srow_m = nc.alloc_sbuf_tensor("srow_m", [1, FS], F32).ap()
    srow_q = nc.alloc_sbuf_tensor("srow_q", [1, FS], F32).ap()
    srow_t = nc.alloc_sbuf_tensor("srow_t", [1, FS], F32).ap()
    srow_r = nc.alloc_sbuf_tensor("srow_r", [1, FS], F32).ap()
    E4 = [nc.alloc_sbuf_tensor(f"E4_{i}", [128, 512], F32).ap() for i in range(2)]
    PT4 = [nc.alloc_sbuf_tensor(f"PT4_{i}", [128, 512], F32).ap() for i in range(2)]
    nm4 = [nc.alloc_sbuf_tensor(f"nm4_{i}", [128, 4], F32).ap() for i in range(3)]
    sm4 = [nc.alloc_sbuf_tensor(f"sm4_{i}", [128, 4], F32).ap() for i in range(3)]
    rg = nc.alloc_sbuf_tensor("rg", [128, 2, HD], F32).ap()
    a_sb = nc.alloc_sbuf_tensor("a_sb", [128, AS], F32).ap()
    a_sb2 = nc.alloc_sbuf_tensor("a_sb2", [128, AS], F32).ap()
    asb2 = [a_sb, a_sb2]
    B_sb = nc.alloc_sbuf_tensor("B_sb", [128, HD, 128], F32).ap()
    identF = nc.alloc_sbuf_tensor("identF", [128, 128], F32).ap()
    onesc = nc.alloc_sbuf_tensor("onesc_s", [128, 1], F32).ap()
    epsc = nc.alloc_sbuf_tensor("epsc_s", [128, 1], F32).ap()
    onesr = nc.alloc_sbuf_tensor("onesr_s", [1, 128], F32).ap()
    bqs = nc.alloc_sbuf_tensor("bqs", [128, H24], F32).ap()
    bps = nc.alloc_sbuf_tensor("bps", [128, C8], F32).ap()

    pb = [nc.alloc_psum_tensor(f"pb{i}", [128, 512], F32).ap() for i in range(8)]

    tok_x = []
    for s in range(4):
        tok_x.append(b.ld(lambda e, s=s: e.dma_start(
            out=x_sb[:, :, s * FS:(s + 1) * FS],
            in_=xT_h.rearrange("(c p) t -> p c t", p=128)[:, :, s * FS:(s + 1) * FS])))
    tok_B = b.ld(lambda e: e.dma_start(out=B_sb, in_=bias_h))
    tok_id = b.ld(lambda e: e.dma_start(out=identF, in_=ident_h))
    tok_oc = b.ld(lambda e: e.dma_start(out=onesc, in_=onesc_h))
    tok_ep = b.ld(lambda e: e.dma_start(out=epsc, in_=epsc_h))
    tok_or = b.ld(lambda e: e.dma_start(out=onesr, in_=onesr_h))
    tok_bq = b.ld(lambda e: e.dma_start(out=bqs, in_=bq_h))
    tok_bp = b.ld(lambda e: e.dma_start(out=bps, in_=bp_h))
    tok_bqs = b.v(lambda e: e.tensor_scalar_mul(bqs[:, 0:C8], bqs[:, 0:C8], 0.125),
                  waits=[tok_bq])

    ln_state = {"rel": [], "scr_rd": [None, None]}

    def emit_ln(col0, width, xnh_, xnl_, cfg, x_waits):
        st_ps, sq_ps, mb_ps, rb_ps, same_bank = cfg
        sl = slice(col0, col0 + width)
        wsl = slice(0, width)
        rel_prev = list(ln_state["rel"])
        last_sq = None
        for c in range(C8):
            sc_buf = scrq[c % 2]
            tsq = b.s(lambda e, c=c, sc_buf=sc_buf: e.activation(
                sc_buf[:, wsl], x_sb[:, c, sl], AF.Square),
                waits=list(x_waits) + [ln_state["scr_rd"][c % 2]])
            last_sq = b.t(lambda e, c=c, sc_buf=sc_buf: e.matmul(
                sq_ps, onesc, sc_buf[:, wsl],
                start=(c == 0), stop=(c == C8 - 1)),
                waits=[tsq, tok_oc] + (rel_prev if c == 0 else []))
            ln_state["scr_rd"][c % 2] = last_sq
        st_mm = None
        for c in range(C8):
            st_mm = b.t(lambda e, c=c: e.matmul(
                st_ps, onesc, x_sb[:, c, sl],
                start=(c == 0), stop=(c == C8 - 1)), waits=list(x_waits) + [tok_oc])
        tv1 = b.v(lambda e: e.tensor_scalar_mul(srow_m[:, wsl], st_ps, 1.0 / 1024.0),
                  waits=[st_mm])
        tv2 = b.v(lambda e: e.tensor_scalar_mul(srow_q[:, wsl], sq_ps, 1.0 / 1024.0),
                  waits=[st_mm])
        tv3 = b.v(lambda e: e.tensor_mul(srow_t[:, wsl], srow_m[:, wsl], srow_m[:, wsl]),
                  waits=[tv1])
        tv4 = b.v(lambda e: e.tensor_sub(srow_q[:, wsl], srow_q[:, wsl], srow_t[:, wsl]),
                  waits=[tv2, tv3])
        ts1 = b.s(lambda e: e.activation(srow_t[:, wsl], srow_q[:, wsl], AF.Ln,
                                         bias=epsc[0:1, :]), waits=[tv4, tok_ep])
        ts2 = b.s(lambda e: e.activation(srow_r[:, wsl], srow_t[:, wsl], AF.Exp,
                                         scale=-0.5), waits=[ts1])
        tb1 = b.t(lambda e: e.matmul(mb_ps, onesr, srow_m[:, wsl],
                                     start=True, stop=True),
                  waits=[tv3, tv1, tv2, tok_or])
        te1 = b.v(lambda e: e.tensor_copy(mb[:, wsl], mb_ps), waits=[tb1])
        tb2 = b.t(lambda e: e.matmul(rb_ps, onesr, srow_r[:, wsl],
                                     start=True, stop=True),
                  waits=[ts2, tv2] + ([te1] if same_bank else []))
        te2 = b.v(lambda e: e.tensor_copy(rb[:, wsl], rb_ps), waits=[tb2])
        toks = []
        for c in range(C8):
            sc_buf = scrq[c % 2]
            t1 = b.v(lambda e, c=c, sc_buf=sc_buf: e.tensor_sub(
                sc_buf[:, wsl], x_sb[:, c, sl], mb[:, wsl]), waits=[te1])
            t2 = b.v(lambda e, sc_buf=sc_buf: e.tensor_mul(
                sc_buf[:, wsl], sc_buf[:, wsl], rb[:, wsl]), waits=[te2])
            t3 = b.v(lambda e, c=c, sc_buf=sc_buf: e.tensor_copy(
                xnh_[:, c, wsl], sc_buf[:, wsl]), waits=[t2])
            t4 = b.v(lambda e, c=c, sc_buf=sc_buf: e.scalar_tensor_tensor(
                xnl_[:, c, wsl], xnh_[:, c, wsl], -1.0, sc_buf[:, wsl],
                ALU.mult, ALU.add), waits=[t3])
            ln_state["scr_rd"][c % 2] = t4
            toks.append(t4)
        ln_state["rel"] = [tv1, tv2, te1, te2]
        return toks

    def ffn_ln_cfg(width):
        return (pb[6][0:1, 0:width], pb[7][0:1, 0:width],
                pb[6][:, 0:width], pb[7][:, 0:width], False)

    def attn_ln_cfg(width):
        return (pb[7][0:1, 0:width], pb[7][0:1, 256:256 + width],
                pb[7][:, 0:width], pb[7][:, 256:256 + width], True)

    state = dict(gu_free=[[None], [None]], dn_free=[[None], [None]],
                 store_gu={}, store_d={})

    def emit_ffn(first, x_waits_fn, arena_waits):
        g_free = [None, None]
        u_free = [None, None]
        d_free = [None, None]
        gu_buf = state["gu_free"]
        dn_buf = state["dn_free"]
        h_free = None
        res_all = []
        xn_cur = emit_ln(0, FS, xn[0][0], xn[0][1], ffn_ln_cfg(FS), x_waits_fn(0))
        for s in range(4):
            sl = slice(s * FS, (s + 1) * FS)
            par_s = s % 2
            xnh_, xnl_ = xn[par_s]
            h_toks = []
            for m in range(H24):
                par = m % 2
                if first and s == 0:
                    lg = b.gld(lambda e, m=m, par=par: e.dma_start(
                        out=wgu[par][:, :, 0, :],
                        in_=wgT_h[:, m * 128:(m + 1) * 128].rearrange("(k p) n -> p k n", p=128)),
                        waits=[gu_buf[par]], ch=("gA" if par == 0 else "gB"))
                    lu = b.gld(lambda e, m=m, par=par: e.dma_start(
                        out=wgu[par][:, :, 1, :],
                        in_=wuT_h[:, m * 128:(m + 1) * 128].rearrange("(k p) n -> p k n", p=128)),
                        waits=[gu_buf[par]], ch=("gC" if par == 0 else "gD"))
                    sg = b.ld(lambda e, m=m, par=par: e.dma_start(
                        out=wgu_bf_h[m],
                        in_=wgu[par].rearrange("p k two n -> p (k two n)")),
                        waits=[lg, lu], ch="sA")
                    state["store_gu"][m] = sg
                    wtok = [lg, lu]
                    buf_rel = [sg]
                else:
                    lw = b.ld(lambda e, m=m, par=par: e.dma_start(
                        out=wgu[par].rearrange("p k two n -> p (k two n)"), in_=wgu_bf_h[m]),
                        waits=[gu_buf[par], state["store_gu"].get(m)] +
                              (arena_waits if (s == 0 and m < 2) else []), ch=f"w{par}")
                    wtok = [lw]
                    buf_rel = []
                last_g = None
                gw = wtok + [g_free[par]] + (xn_cur if m == 0 else [])
                for pi, xs in enumerate((xnh_, xnl_)):
                    for k in range(C8):
                        last_g = b.t(lambda e, par=par, k=k, xs=xs, pi=pi, sl=sl: e.matmul(
                            pb[par][:, :], wgu[par][:, k, 0, :], xs[:, k, :],
                            start=(pi == 0 and k == 0), stop=(pi == 1 and k == C8 - 1)),
                            waits=gw if (pi == 0 and k == 0) else [],
                            inc=(pi == 1 and k == C8 - 1))
                last_u = None
                uw = [u_free[par]]
                for pi, xs in enumerate((xnh_, xnl_)):
                    for k in range(C8):
                        last_u = b.t(lambda e, par=par, k=k, xs=xs, pi=pi: e.matmul(
                            pb[2 + par][:, :], wgu[par][:, k, 1, :], xs[:, k, :],
                            start=(pi == 0 and k == 0), stop=(pi == 1 and k == C8 - 1)),
                            waits=uw if (pi == 0 and k == 0) else [],
                            inc=(pi == 1 and k == C8 - 1))
                gu_buf[par] = [last_u] + buf_rel
                tsil = b.s(lambda e, par=par: e.activation(
                    sA[par], pb[par], AF.Silu, scale=S127),
                    waits=[last_g, u_free[par]])
                g_free[par] = tsil
                th1 = b.v(lambda e, par=par: e.tensor_mul(scr_h, sA[par], pb[2 + par]),
                          waits=[tsil, last_u, h_free])
                u_free[par] = th1
                th2 = b.v(lambda e, m=m: e.tensor_copy(hh[:, m, :], scr_h), waits=[th1])
                th3 = b.v(lambda e, m=m: e.scalar_tensor_tensor(
                    hl[:, m, :], hh[:, m, :], -1.0, scr_h, ALU.mult, ALU.add), waits=[th2])
                h_toks.append(th3)
            if s < 3:
                xn_next = emit_ln((s + 1) * FS, FS, xn[1 - par_s][0], xn[1 - par_s][1],
                                  ffn_ln_cfg(FS), x_waits_fn(s + 1))
            sub_res = []
            last_d = None
            for m2 in range(C8):
                par = m2 % 2
                if first and s == 0:
                    ldd = b.gld(lambda e, m2=m2, par=par: e.dma_start(
                        out=wdn[par],
                        in_=wdT_h[:, m2 * 128:(m2 + 1) * 128].rearrange("(k p) n -> p k n", p=128)),
                        waits=[dn_buf[par]], ch=("gE" if par == 0 else "gF"))
                    sd = b.ld(lambda e, m2=m2, par=par: e.dma_start(
                        out=wd_bf_h[m2],
                        in_=wdn[par].rearrange("p k n -> p (k n)")),
                        waits=[ldd], ch="sB")
                    state["store_d"][m2] = sd
                    wtok = [ldd]
                    buf_rel = [sd]
                else:
                    lw = b.ld(lambda e, m2=m2, par=par: e.dma_start(
                        out=wdn[par].rearrange("p k n -> p (k n)"), in_=wd_bf_h[m2]),
                        waits=[dn_buf[par], state["store_d"].get(m2)] +
                              (arena_waits if (s == 0 and m2 < 2) else []), ch=f"d{par}")
                    wtok = [lw]
                    buf_rel = []
                dw = wtok + [d_free[par]] + (h_toks if m2 == 0 else [])
                for pi, hs in enumerate((hh, hl)):
                    for k in range(H24):
                        last_d = b.t(lambda e, par=par, k=k, hs=hs, pi=pi: e.matmul(
                            pb[4 + par][:, :], wdn[par][:, k, :], hs[:, k, :],
                            start=(pi == 0 and k == 0), stop=(pi == 1 and k == H24 - 1)),
                            waits=dw if (pi == 0 and k == 0) else [],
                            inc=(pi == 1 and k == H24 - 1))
                dn_buf[par] = [last_d] + buf_rel
                tres = b.v(lambda e, par=par, m2=m2, sl=sl: e.scalar_tensor_tensor(
                    x_sb[:, m2, sl], pb[4 + par], DOWN_SCALE, x_sb[:, m2, sl],
                    ALU.mult, ALU.add), waits=[last_d])
                d_free[par] = tres
                sub_res.append(tres)
            h_free = last_d
            res_all.extend(sub_res)
            if s < 3:
                xn_cur = xn_next
        return res_all

    attn_state = dict(bank6=None, pt=[None, None], av=[None, None],
                      sbank=[None, None], qmm=[None, None, None, None], pj_mm=None,
                      wq_store={}, wp_store={}, f32_rel=[], pjslot=[None, None])

    def emit_attn_subtile(s8, xn_cur, x_waits_ln, arena_waits):
        asl = slice(s8 * AS, (s8 + 1) * AS)
        wsl = slice(0, AS)
        par_s = s8 % 2
        split = (s8 == 0)
        ev_toks = []
        qb_free = list(attn_state["qmm"])
        ps_free = [None, None]
        f32_free = [None, None]
        nslot = 2 if split else 4
        pend_tlo = {}

        def emit_sw(mm, par):
            sw = b.ld(lambda e, mm=mm, par=par: e.dma_start(
                out=wq_bf_h[mm], in_=wqb[par].rearrange("p k two n -> p (k two n)")),
                waits=[pend_tlo.pop(mm)], ch=f"s{par}")
            attn_state["wq_store"][mm] = sw
            return sw

        for m in range(H24):
            par = m % nslot
            if split:
                lwf = b.ld(lambda e, m=m, par=par: e.dma_start(
                    out=wf32[par],
                    in_=wqkv_h[:, m * 128:(m + 1) * 128].rearrange("(k p) n -> p k n", p=128)),
                    waits=[f32_free[par]] + (arena_waits if m < 4 else []), ch=f"d{par}")
                if m - 2 in pend_tlo:
                    qb_free[par] = [qb_free[par], emit_sw(m - 2, par)]
                thi = b.v(lambda e, par=par: e.tensor_copy(
                    wqb[par][:, :, 0, :], wf32[par]),
                    waits=[lwf, qb_free[par]] + (arena_waits if m < 2 else []))
                tlo = b.v(lambda e, par=par: e.scalar_tensor_tensor(
                    wqb[par][:, :, 1, :], wqb[par][:, :, 0, :], -1.0, wf32[par],
                    ALU.mult, ALU.add), waits=[thi])
                f32_free[par] = tlo
                pend_tlo[m] = tlo
                attn_state.setdefault("f32_rel", [])
                wtok = [tlo]
                buf_rel = None
            else:
                lw = b.ld(lambda e, m=m, par=par: e.dma_start(
                    out=wslots[par].rearrange("p k two n -> p (k two n)"), in_=wq_bf_h[m]),
                    waits=[qb_free[par], attn_state["wq_store"].get(m)] +
                          (attn_state["f32_rel"] if (par >= 2 and m < 4) else []) +
                          (arena_waits if m < 2 else []), ch=f"w{par}")
                wtok = [lw]
                buf_rel = None
            last = None
            mw = wtok + [ps_free[par % 2]] + (xn_cur if m == 0 else []) + \
                ([attn_state["sbank"][par % 2]] if m < 2 else [])
            for pi, (xs, wi) in enumerate([(0, 0), (1, 0), (0, 1)]):
                xbuf = xn[par_s][xs]
                for k in range(C8):
                    last = b.t(lambda e, par=par, k=k, xbuf=xbuf, wi=wi, pi=pi: e.matmul(
                        pb[par % 2][:, 0:AS], wslots[par][:, k, wi, :], xbuf[:, k, wsl],
                        start=(pi == 0 and k == 0), stop=(pi == 2 and k == C8 - 1)),
                        waits=mw if (pi == 0 and k == 0) else [],
                        inc=(pi == 2 and k == C8 - 1))
            qb_free[par] = last if buf_rel is None else [last, buf_rel]
            tev = b.s(lambda e, m=m, par=par: e.activation(
                qkv_sb[:, m, :], pb[par % 2][:, 0:AS], AF.Identity,
                bias=bqs[:, m:m + 1], scale=(0.125 if m < C8 else 1.0)),
                waits=[last, tok_bqs])
            ps_free[par % 2] = tev
            ev_toks.append(tev)
        if split:
            for mm in (H24 - 2, H24 - 1):
                p_ = mm % 2
                tl = pend_tlo[mm]
                sw = emit_sw(mm, p_)
                qb_free[p_] = [qb_free[p_], sw]
                attn_state["f32_rel"] = attn_state["f32_rel"][-4:] + [tl, sw]
        attn_state["qmm"] = list(qb_free) + ([None, None] if split else [])
        if s8 < (T // AS) - 1:
            xn_next = emit_ln((s8 + 1) * AS, AS, xn[1 - par_s][0], xn[1 - par_s][1],
                              attn_ln_cfg(AS), x_waits_ln)
        else:
            xn_next = None
        b6 = attn_state["bank6"]
        g_toks = []
        sbanks = [pb[0], pb[1], pb[7]]
        relcur = list(ln_state["rel"])
        s_free = [ev_toks[22], ev_toks[23], relcur]
        e4_free = [[], [], []]
        E4x = E4 + [sA[0]]
        pt_free = attn_state["pt"]
        av_free = attn_state["av"]
        vt_ev = [[None, None], [None, None]]
        for g in range(2):
            gsl = slice(g * 128, (g + 1) * 128)
            for r in range(2):
                lastt = None
                for j in range(4):
                    jc = r * 4 + j
                    lastt = b.t(lambda e, jc=jc, j=j, gsl=gsl: e.transpose(
                        pb[6][:, j * 128:(j + 1) * 128], qkv_sb[:, 16 + jc, gsl], identF),
                        waits=[ev_toks[16 + r * 4 + 3], tok_id, b6] if j == 0 else [])
                tv = b.v(lambda e, r=r, g=g: e.tensor_copy(
                    VT[:, g, r * 512:(r + 1) * 512], pb[6][:, 0:512]), waits=[lastt])
                b6 = tv
                vt_ev[g][r] = tv
        tex_last = {}
        tpe_tok = {}
        texp_tok = {}
        av_g_last = {0: [], 1: []}

        def wave_heads(w):
            return [8 * (w // 2) + (w % 2) + 2 * j for j in range(4)]

        def emit_qk(i):
            g, w = i // 4, i % 4
            p3 = i % 3
            bnk = sbanks[p3]
            gsl = slice(g * 128, (g + 1) * 128)
            heads = wave_heads(w)
            off = 64 * (w % 2)
            last = None
            for j in range(4):
                h = heads[j]
                jcq = h // 2
                last = b.t(lambda e, j=j, jcq=jcq, off=off, gsl=gsl, bnk=bnk: e.matmul(
                    bnk[:, j * 128:(j + 1) * 128],
                    qkv_sb[off:off + 64, jcq, gsl],
                    qkv_sb[off:off + 64, 8 + jcq, gsl], start=True, stop=True),
                    waits=[ev_toks[23], s_free[p3]] if j == 0 else [])
            bsl = B_sb.rearrange("p (a two) b -> p two a b", two=2)[
                :, w % 2, 4 * (w // 2):4 * (w // 2) + 4, :]
            tba = b.v(lambda e, bsl=bsl, bnk=bnk: e.tensor_add(
                bnk[:, 0:512], bnk[:, 0:512], bsl),
                waits=[last, tok_B])
            tmx = b.v(lambda e, p3=p3, bnk=bnk: e.reduce_max(
                nm4[p3], bnk[:, 0:512].rearrange("p (a b) -> p a b", b=128),
                axis=AX, negate=True), waits=[tba])
            texl = None
            for j in range(4):
                texl = b.s(lambda e, j=j, p3=p3, bnk=bnk: e.activation(
                    E4x[p3][:, j * 128:(j + 1) * 128],
                    bnk[:, j * 128:(j + 1) * 128], AF.Exp,
                    bias=nm4[p3][:, j:j + 1],
                    accum_out=sm4[p3][:, j:j + 1]),
                    waits=([tmx] + e4_free[p3]) if j == 0 else [])
            s_free[p3] = texl
            tex_last[i] = texl
            rsl = rg.rearrange("p g (a two) -> p g two a", two=2)[
                :, g, w % 2, 4 * (w // 2):4 * (w // 2) + 4]
            tln = b.s(lambda e, rsl=rsl, p3=p3: e.activation(
                rsl, sm4[p3], AF.Ln), waits=[texl])
            texp = b.s(lambda e, rsl=rsl: e.activation(
                rsl, rsl, AF.Exp, scale=-1.0), waits=[tln])
            texp_tok[i] = texp
            e4_free[p3] = []

        def emit_pt(i):
            par = i % 2
            p3 = i % 3
            lastt = None
            for j in range(4):
                lastt = b.t(lambda e, j=j, par=par, p3=p3: e.transpose(
                    pb[2 + par][:, j * 128:(j + 1) * 128],
                    E4x[p3][:, j * 128:(j + 1) * 128], identF),
                    waits=[tex_last[i], pt_free[par], tok_id] if j == 0 else [])
            tpe = b.s(lambda e, par=par: e.activation(
                PT4[par], pb[2 + par][:, 0:512], AF.Identity), waits=[lastt])
            pt_free[par] = tpe
            tpe_tok[i] = tpe
            e4_free[p3].append(lastt)

        def emit_av(i):
            g, w, par = i // 4, i % 4, i % 2
            heads = wave_heads(w)
            lasta = None
            for j in range(4):
                h = heads[j]
                lasta = b.t(lambda e, j=j, h=h, g=g, par=par: e.matmul(
                    pb[4 + par][:, j * 64:(j + 1) * 64],
                    PT4[par][:, j * 128:(j + 1) * 128],
                    VT[:, g, h * 64:(h + 1) * 64], start=True, stop=True),
                    waits=[tpe_tok[i], vt_ev[g][w // 2], av_free[par]] if j == 0 else [])
            tosl = None
            for j in range(4):
                h = heads[j]
                tosl = b.v(lambda e, j=j, h=h, g=g, par=par: e.tensor_scalar_mul(
                    O_sb[:, g, h * 64:(h + 1) * 64], pb[4 + par][:, j * 64:(j + 1) * 64],
                    rg[:, g, h:h + 1]),
                    waits=[lasta, texp_tok[i]] if j == 0 else [])
            av_free[par] = tosl
            av_g_last[g] = (av_g_last[g] + [tosl])[-2:]

        ot_rd = [[], []]

        def emit_ot(g):
            nonlocal b6
            gsl = slice(g * 128, (g + 1) * 128)
            for r in range(2):
                lastt = None
                for j in range(4):
                    jc = r * 4 + j
                    lastt = b.t(lambda e, jc=jc, j=j, g=g: e.transpose(
                        pb[6][:, j * 128:(j + 1) * 128],
                        O_sb[:, g, jc * 128:(jc + 1) * 128], identF),
                        waits=av_g_last[g] + [b6] if j == 0 else [])
                ot_rd[g].append(lastt)
                t1 = b.v(lambda e, r=r, gsl=gsl: e.tensor_copy(
                    oTh[:, r * 4:(r + 1) * 4, gsl],
                    pb[6][:, 0:512].rearrange("p (a b) -> p a b", b=128)), waits=[lastt])
                t2 = b.v(lambda e, r=r, gsl=gsl: e.scalar_tensor_tensor(
                    oTl[:, r * 4:(r + 1) * 4, gsl], oTh[:, r * 4:(r + 1) * 4, gsl], -1.0,
                    pb[6][:, 0:512].rearrange("p (a b) -> p a b", b=128),
                    ALU.mult, ALU.add), waits=[t1])
                b6 = t2
                g_toks.append(t2)

        for i in range(10):
            if i < 8:
                emit_qk(i)
            if 1 <= i <= 8:
                emit_pt(i - 1)
            if 2 <= i <= 9:
                emit_av(i - 2)
            if i - 2 == 3:
                emit_ot(0)
        emit_ot(1)
        attn_state["sbank"] = [s_free[0], s_free[1]]
        ln_state["rel"] = list(ln_state["rel"]) + [tex_last[5]]
        attn_state["pt"] = pt_free
        attn_state["av"] = av_free
        res_toks = []
        pj_free = [None, None]
        wb_free = list(qb_free) + ([None, None] if split else [])
        f32_free = [None, None]
        last_mm = attn_state["pj_mm"]
        pw_free = list(attn_state["pjslot"])
        pend_ptlo = {}

        def emit_psw(mm, par):
            sw = b.ld(lambda e, mm=mm, par=par: e.dma_start(
                out=wp_bf_h[mm], in_=wpb[par].rearrange("p k two n -> p (k two n)")),
                waits=[pend_ptlo.pop(mm)], ch=f"s{par}")
            attn_state["wp_store"][mm] = sw
            return sw

        for m2 in range(C8):
            par = m2 % nslot if split else m2 % 2
            if split:
                lwf = b.ld(lambda e, m2=m2, par=par: e.dma_start(
                    out=wf32[par],
                    in_=wproj_h[:, m2 * 128:(m2 + 1) * 128].rearrange("(k p) n -> p k n", p=128)),
                    waits=[f32_free[par]] + ([ev_toks[-1]] if m2 < 2 else []), ch=f"d{par}")
                if m2 - 2 in pend_ptlo:
                    wb_free[par] = [wb_free[par], emit_psw(m2 - 2, par)]
                thi = b.v(lambda e, par=par: e.tensor_copy(
                    wpb[par][:, :, 0, :], wf32[par]),
                    waits=[lwf, wb_free[par]] + ([attn_state["qmm"][par]] if m2 < 2 else []))
                tlo = b.v(lambda e, par=par: e.scalar_tensor_tensor(
                    wpb[par][:, :, 1, :], wpb[par][:, :, 0, :], -1.0, wf32[par],
                    ALU.mult, ALU.add), waits=[thi])
                f32_free[par] = tlo
                pend_ptlo[m2] = tlo
                wtok = [tlo]
                buf_rel = None
            else:
                lw = b.ld(lambda e, m2=m2, par=par: e.dma_start(
                    out=pw[par].rearrange("p k two n -> p (k two n)"), in_=wp_bf_h[m2]),
                    waits=[pw_free[par], attn_state["wp_store"].get(m2)] +
                          (ot_rd[par] if m2 < 2 else []), ch=f"d{par}")
                wtok = [lw]
                buf_rel = None
            bnk = pb[6] if m2 % 2 == 0 else pb[2]
            pjw = pj_free[m2 % 2]
            if m2 < 2:
                pjw = [pj_free[m2 % 2], b6] + (g_toks if m2 == 0 else []) + \
                    ([pt_free[0]] if m2 == 1 else [])
            last = None
            mw = wtok + [pjw]
            wsrc = wslots[par] if split else pw[par]
            for pi, (xs, wi) in enumerate([(oTh, 0), (oTl, 0), (oTh, 1)]):
                for k in range(C8):
                    last = b.t(lambda e, wsrc=wsrc, k=k, xs=xs, wi=wi, pi=pi, bnk=bnk: e.matmul(
                        bnk[:, 0:AS], wsrc[:, k, wi, :], xs[:, k, :],
                        start=(pi == 0 and k == 0), stop=(pi == 2 and k == C8 - 1)),
                        waits=mw if (pi == 0 and k == 0) else [],
                        inc=(pi == 2 and k == C8 - 1))
            if split:
                wb_free[par] = last if buf_rel is None else [last, buf_rel]
            else:
                pw_free[par] = last
            last_mm = last
            a_half = a_sb[:, 0:AS] if False else a_sb
            tev = b.s(lambda e, m2=m2, bnk=bnk: e.activation(
                asb2[m2 % 2], bnk[:, 0:AS], AF.Identity, bias=bps[:, m2:m2 + 1]),
                waits=[last, tok_bp])
            tres = b.v(lambda e, m2=m2, asl=asl: e.tensor_add(
                x_sb[:, m2, asl], x_sb[:, m2, asl], asb2[m2 % 2]), waits=[tev])
            pj_free[m2 % 2] = tres
            res_toks.append(tres)
        attn_state["pj_mm"] = last_mm
        if split:
            for mm in (C8 - 2, C8 - 1):
                p_ = mm % 2
                tl = pend_ptlo[mm]
                sw = emit_psw(mm, p_)
                wb_free[p_] = [wb_free[p_], sw]
                attn_state["f32_rel"] = attn_state["f32_rel"][-4:] + [tl, sw]
            attn_state["qmm"] = list(wb_free)[:4]
        attn_state["pjslot"] = pw_free
        attn_state["bank6"] = pj_free[0]
        attn_state["pt"][0] = pj_free[1]
        return res_toks, xn_next

    r1 = emit_ffn(first=True,
                  x_waits_fn=lambda s: [tok_x[s]],
                  arena_waits=[])
    cur = r1
    if stage >= 2:
        allres = []
        arena_w = [state["gu_free"][0], state["gu_free"][1],
                   state["dn_free"][0], state["dn_free"][1]]
        xn_cur = emit_ln(0, AS, xn[0][0], xn[0][1], attn_ln_cfg(AS), r1)
        for s8 in range(T // AS):
            rs, xn_cur = emit_attn_subtile(s8, xn_cur, r1,
                                           arena_w if s8 == 0 else [])
            allres.extend(rs)
        cur = allres
    if stage >= 3:
        arena_w2 = list(attn_state["qmm"]) + [attn_state["pj_mm"], attn_state["bank6"]]
        r3 = emit_ffn(first=False,
                      x_waits_fn=lambda s: allres[2 * s * C8:(2 * s + 2) * C8],
                      arena_waits=arena_w2)
        cur = r3

    tok_y = b.ld(lambda e: e.dma_start(
        out=yT_h.rearrange("(c p) t -> p c t", p=128), in_=x_sb), waits=cur, ch="y")
    b.emit("sync", lambda e: e.nop(), waits=[tok_y])

    b.replay(nc)
    return nc


def _bias_tiles(bias_table):
    bt = np.asarray(bias_table, dtype=np.float32)
    q = np.arange(8)
    k = np.arange(8)
    idx = 112 + q[:, None] - k[None, :]
    bias8 = bt[idx.reshape(-1)].reshape(8, 8, HD)
    B_full = np.full((128, HD, 128), NEG, dtype=np.float32)
    for w in range(16):
        B_full[w * 8:(w + 1) * 8, :, w * 8:(w + 1) * 8] = bias8.transpose(0, 2, 1)
    return B_full


def kernel(L0_attn_x, L0_attn_w_qkv, L0_attn_b_qkv, L0_attn_w_proj, L0_attn_b_proj,
           L0_attn_bias_table, L0_ffn_w_gate, L0_ffn_w_up, L0_ffn_w_down):
    x = np.asarray(L0_attn_x)
    xr = np.roll(x, -SHIFT, axis=1).reshape(B_ * N_, C_)
    shards = [np.ascontiguousarray(xr[c * T:(c + 1) * T].T) for c in range(NCORES)]

    wqkv = np.ascontiguousarray(np.asarray(L0_attn_w_qkv, dtype=np.float32))
    wproj = np.ascontiguousarray(np.asarray(L0_attn_w_proj, dtype=np.float32))
    wgT = np.ascontiguousarray(np.asarray(L0_ffn_w_gate).T)
    wuT = np.ascontiguousarray(np.asarray(L0_ffn_w_up).T)
    wdT = np.ascontiguousarray(np.asarray(L0_ffn_w_down).T)
    bq = np.ascontiguousarray(np.asarray(L0_attn_b_qkv, dtype=np.float32).reshape(H24, 128).T)
    bp = np.ascontiguousarray(np.asarray(L0_attn_b_proj, dtype=np.float32).reshape(C8, 128).T)
    Bt = _bias_tiles(L0_attn_bias_table)
    ident = np.eye(128, dtype=np.float32)
    onesc = np.ones((128, 1), dtype=np.float32)
    epsc = np.full((128, 1), EPS, dtype=np.float32)
    onesr = np.ones((1, 128), dtype=np.float32)

    nc = build_nc(STAGE)
    common = dict(wqkv=wqkv, wproj=wproj, wgT=wgT, wuT=wuT, wdT=wdT,
                  bq=bq, bp=bp, biasm=Bt, ident=ident, onesc=onesc, onesr=onesr,
                  epsc=epsc)
    in_maps = [dict(common, xT=shards[c]) for c in range(NCORES)]
    res = run_bass_kernel_spmd(nc, in_maps, core_ids=list(range(NCORES)))
    out = np.concatenate([np.asarray(res.results[c]["yT"]).T for c in range(NCORES)], axis=0)
    return np.roll(out.reshape(B_, N_, C_), SHIFT, axis=1)


# version 18 (final)
# speedup vs baseline: 1.0131x; 1.0131x over previous
import numpy as np
import ml_dtypes
import concourse.bass as bass
import concourse.mybir as mybir
from concourse.bass_utils import run_bass_kernel_spmd

F32 = mybir.dt.float32
BF16 = mybir.dt.bfloat16
I32 = mybir.dt.int32
AX = mybir.AxisListType.X
AF = mybir.ActivationFunctionType
ALU = mybir.AluOpType

B_, N_, C_ = 4, 4096, 1024
NCORES = 8
T = (B_ * N_) // NCORES
C8 = 8
H24 = 24
HD = 16
FS = 512
AS = 256
SHIFT = 4
S127 = float(np.float32(1.0 / 127.0))
DOWN_SCALE = float(np.float32(0.5) * np.float32(1.0 / 127.0) * np.float32(1.0 / 127.0))
EPS = 1e-5
NEG = -1e9

STAGE = 3


class Builder:
    ENGINES = ("tensor", "vector", "scalar", "sync", "gpsimd")

    def __init__(self):
        self.ops = {e: [] for e in self.ENGINES}
        self.cnt = {e: 0 for e in self.ENGINES}
        self.chan = {}

    def emit(self, eng, fn, waits=(), inc=False, ch=None):
        ws = []
        for w in waits:
            if w is None:
                continue
            if isinstance(w, list):
                ws.extend(x for x in w if x is not None)
            else:
                ws.append(w)
        if ch is not None:
            prev = self.chan.get(ch, 0)
            if prev:
                ws.append((ch, prev))
            self.chan[ch] = prev + 16
            self.ops[eng].append((ws, fn, False, ch))
            return (ch, prev + 16)
        self.ops[eng].append((ws, fn, inc, None))
        if inc:
            self.cnt[eng] += 1
            return (eng, self.cnt[eng])
        return None

    def t(self, fn, waits=(), inc=True):
        return self.emit("tensor", fn, waits, inc=inc)

    def v(self, fn, waits=(), inc=True):
        return self.emit("vector", fn, waits, inc=inc)

    def s(self, fn, waits=(), inc=True):
        return self.emit("scalar", fn, waits, inc=inc)

    def ld(self, fn, waits=(), ch="c0"):
        return self.emit("sync", fn, waits, ch=ch)

    def gld(self, fn, waits=(), ch="g0"):
        return self.emit("gpsimd", fn, waits, ch=ch)

    def replay(self, nc):
        import contextlib
        stack = contextlib.ExitStack()
        sems = {}
        for e in self.ENGINES:
            sems[e] = stack.enter_context(nc.semaphore("s_" + e))
        for ch in self.chan:
            sems[ch] = stack.enter_context(nc.semaphore("sch_" + ch))
        block = stack.enter_context(nc.Block())

        def run_engine(ename):
            def body(eng):
                waited = {}
                for ws, fn, inc, ch in self.ops[ename]:
                    for key, val in ws:
                        if key == ename:
                            continue
                        if waited.get(key, 0) >= val:
                            continue
                        eng.wait_ge(sems[key], val)
                        waited[key] = val
                    inst = fn(eng)
                    if ch is not None:
                        inst.then_inc(sems[ch], 16)
                    elif inc:
                        inst.then_inc(sems[ename], 1)
            return body

        block.tensor(run_engine("tensor"))
        block.vector(run_engine("vector"))
        block.scalar(run_engine("scalar"))
        block.sync(run_engine("sync"))
        block.gpsimd(run_engine("gpsimd"))
        stack.close()


def build_nc(stage=STAGE):
    nc = bass.Bass("TRN2", target_bir_lowering=False, detect_race_conditions=False)
    b = Builder()

    xT_h = nc.dram_tensor("xT", [C_, T], F32, kind="ExternalInput").ap()
    wqkv_h = nc.dram_tensor("wqkv", [C_, 3 * C_], F32, kind="ExternalInput").ap()
    wproj_h = nc.dram_tensor("wproj", [C_, C_], F32, kind="ExternalInput").ap()
    wgT_h = nc.dram_tensor("wgT", [C_, 3 * C_], I32, kind="ExternalInput").ap()
    wuT_h = nc.dram_tensor("wuT", [C_, 3 * C_], I32, kind="ExternalInput").ap()
    wdT_h = nc.dram_tensor("wdT", [3 * C_, C_], I32, kind="ExternalInput").ap()
    bq_h = nc.dram_tensor("bq", [128, H24], F32, kind="ExternalInput").ap()
    bp_h = nc.dram_tensor("bp", [128, C8], F32, kind="ExternalInput").ap()
    bias_h = nc.dram_tensor("biasm", [128, HD, 128], F32, kind="ExternalInput").ap()
    ident_h = nc.dram_tensor("ident", [128, 128], F32, kind="ExternalInput").ap()
    onesc_h = nc.dram_tensor("onesc", [128, 1], F32, kind="ExternalInput").ap()
    epsc_h = nc.dram_tensor("epsc", [128, 1], F32, kind="ExternalInput").ap()
    onesr_h = nc.dram_tensor("onesr", [1, 128], F32, kind="ExternalInput").ap()
    yT_h = nc.dram_tensor("yT", [C_, T], F32, kind="ExternalOutput").ap()
    wgu_bf_h = nc.dram_tensor("wgu_bf", [H24, 128, 2048], BF16).ap()
    wd_bf_h = nc.dram_tensor("wd_bf", [C8, 128, 3072], BF16).ap()
    wq_bf_h = nc.dram_tensor("wq_bf", [H24, 128, 2048], BF16).ap()
    wp_bf_h = nc.dram_tensor("wp_bf", [C8, 128, 2048], BF16).ap()

    x_sb = nc.alloc_sbuf_tensor("x_sb", [128, C8, T], F32).ap()
    xn = [(nc.alloc_sbuf_tensor(f"xnh{i}", [128, C8, FS], BF16).ap(),
           nc.alloc_sbuf_tensor(f"xnl{i}", [128, C8, FS], BF16).ap()) for i in range(2)]
    ha = nc.alloc_sbuf_tensor("ha", [128, H24 * FS], BF16)
    hb = nc.alloc_sbuf_tensor("hb", [128, H24 * FS], BF16)
    ha_ad = nc.lookup_mloc(ha).addr
    hb_ad = nc.lookup_mloc(hb).addr
    hh = nc.alloc_sbuf_tensor_at("hh", [128, H24, FS], BF16, offset=ha_ad).ap()
    hl = nc.alloc_sbuf_tensor_at("hl", [128, H24, FS], BF16, offset=hb_ad).ap()
    qkv_sb = nc.alloc_sbuf_tensor_at("qkv_sb", [128, H24, AS], F32, offset=ha_ad).ap()
    VT = nc.alloc_sbuf_tensor_at("VT", [128, 2, 1024], F32, offset=hb_ad).ap()
    O_sb = nc.alloc_sbuf_tensor_at("O_sb", [128, 2, 1024], F32, offset=hb_ad + 8192).ap()
    oTh = nc.alloc_sbuf_tensor_at("oTh", [128, C8, AS], BF16, offset=hb_ad + 16384).ap()
    oTl = nc.alloc_sbuf_tensor_at("oTl", [128, C8, AS], BF16, offset=hb_ad + 20480).ap()
    wa = nc.alloc_sbuf_tensor("wa", [128, 20992], mybir.dt.uint8)
    wa_ad = nc.lookup_mloc(wa).addr
    wgu = [nc.alloc_sbuf_tensor_at(f"wgu{i}", [128, C8, 2, 128], BF16,
                                   offset=wa_ad + i * 4096).ap() for i in range(2)]
    wqb = wgu
    wpb = wgu
    wpair = nc.alloc_sbuf_tensor_at("wpair", [128, 2, 2048], BF16, offset=wa_ad).ap()
    wdn = [nc.alloc_sbuf_tensor_at(f"wdn{i}", [128, H24, 128], BF16,
                                   offset=wa_ad + 8448 + i * 6144).ap() for i in range(2)]
    wdpair = nc.alloc_sbuf_tensor_at("wdpair", [128, 2, 3072], BF16,
                                     offset=wa_ad + 8448).ap()
    wf32 = [nc.alloc_sbuf_tensor_at(f"wf32{i}", [128, C8, 128], F32,
                                    offset=wa_ad + 8448 + i * 4096).ap() for i in range(2)]
    pw = [nc.alloc_sbuf_tensor_at(f"pw{i}", [128, C8, 2, 128], BF16,
                                  offset=hb_ad + 8192 + i * 4096).ap()
          for i in range(2)]
    wq_x = [nc.alloc_sbuf_tensor_at(f"wq_x{i}", [128, C8, 2, 128], BF16,
                                    offset=wa_ad + 8448 + i * 4096).ap() for i in range(2)]
    wslots = [wgu[0], wgu[1], wq_x[0], wq_x[1]]

    scrq = [nc.alloc_sbuf_tensor(f"scrq{i}", [128, FS], F32).ap() for i in range(2)]
    scr_h = nc.alloc_sbuf_tensor("scr_h", [128, FS], F32).ap()
    sA = [nc.alloc_sbuf_tensor(f"sA{i}", [128, FS], F32).ap() for i in range(2)]
    mb = nc.alloc_sbuf_tensor("mb", [128, FS], F32).ap()
    rb = nc.alloc_sbuf_tensor("rb", [128, FS], F32).ap()
    srow_m = nc.alloc_sbuf_tensor("srow_m", [1, FS], F32).ap()
    srow_q = nc.alloc_sbuf_tensor("srow_q", [1, FS], F32).ap()
    srow_t = nc.alloc_sbuf_tensor("srow_t", [1, FS], F32).ap()
    srow_r = nc.alloc_sbuf_tensor("srow_r", [1, FS], F32).ap()
    E4 = [nc.alloc_sbuf_tensor(f"E4_{i}", [128, 512], F32).ap() for i in range(2)]
    PT4 = [nc.alloc_sbuf_tensor(f"PT4_{i}", [128, 512], F32).ap() for i in range(2)]
    nm4 = [nc.alloc_sbuf_tensor(f"nm4_{i}", [128, 4], F32).ap() for i in range(3)]
    sm4 = [nc.alloc_sbuf_tensor(f"sm4_{i}", [128, 4], F32).ap() for i in range(3)]
    rg = nc.alloc_sbuf_tensor("rg", [128, 2, HD], F32).ap()
    a_sb = nc.alloc_sbuf_tensor("a_sb", [128, AS], F32).ap()
    a_sb2 = nc.alloc_sbuf_tensor("a_sb2", [128, AS], F32).ap()
    asb2 = [a_sb, a_sb2]
    B_sb = nc.alloc_sbuf_tensor("B_sb", [128, HD, 128], F32).ap()
    identF = nc.alloc_sbuf_tensor("identF", [128, 128], F32).ap()
    onesc = nc.alloc_sbuf_tensor("onesc_s", [128, 1], F32).ap()
    epsc = nc.alloc_sbuf_tensor("epsc_s", [128, 1], F32).ap()
    onesr = nc.alloc_sbuf_tensor("onesr_s", [1, 128], F32).ap()
    bqs = nc.alloc_sbuf_tensor("bqs", [128, H24], F32).ap()
    bps = nc.alloc_sbuf_tensor("bps", [128, C8], F32).ap()

    pb = [nc.alloc_psum_tensor(f"pb{i}", [128, 512], F32).ap() for i in range(8)]

    tok_x = []
    for s in range(4):
        tok_x.append(b.ld(lambda e, s=s: e.dma_start(
            out=x_sb[:, :, s * FS:(s + 1) * FS],
            in_=xT_h.rearrange("(c p) t -> p c t", p=128)[:, :, s * FS:(s + 1) * FS])))
    tok_B = b.ld(lambda e: e.dma_start(out=B_sb, in_=bias_h))
    tok_id = b.ld(lambda e: e.dma_start(out=identF, in_=ident_h))
    tok_oc = b.ld(lambda e: e.dma_start(out=onesc, in_=onesc_h))
    tok_ep = b.ld(lambda e: e.dma_start(out=epsc, in_=epsc_h))
    tok_or = b.ld(lambda e: e.dma_start(out=onesr, in_=onesr_h))
    tok_bq = b.ld(lambda e: e.dma_start(out=bqs, in_=bq_h))
    tok_bp = b.ld(lambda e: e.dma_start(out=bps, in_=bp_h))
    tok_bqs = b.v(lambda e: e.tensor_scalar_mul(bqs[:, 0:C8], bqs[:, 0:C8], 0.125),
                  waits=[tok_bq])

    ln_state = {"rel": [], "scr_rd": [None, None]}

    def emit_ln(col0, width, xnh_, xnl_, cfg, x_waits):
        st_ps, sq_ps, mb_ps, rb_ps, same_bank = cfg
        sl = slice(col0, col0 + width)
        wsl = slice(0, width)
        rel_prev = list(ln_state["rel"])
        last_sq = None
        for c in range(C8):
            sc_buf = scrq[c % 2]
            tsq = b.s(lambda e, c=c, sc_buf=sc_buf: e.activation(
                sc_buf[:, wsl], x_sb[:, c, sl], AF.Square),
                waits=list(x_waits) + [ln_state["scr_rd"][c % 2]])
            last_sq = b.t(lambda e, c=c, sc_buf=sc_buf: e.matmul(
                sq_ps, onesc, sc_buf[:, wsl],
                start=(c == 0), stop=(c == C8 - 1)),
                waits=[tsq, tok_oc] + (rel_prev if c == 0 else []))
            ln_state["scr_rd"][c % 2] = last_sq
        st_mm = None
        for c in range(C8):
            st_mm = b.t(lambda e, c=c: e.matmul(
                st_ps, onesc, x_sb[:, c, sl],
                start=(c == 0), stop=(c == C8 - 1)), waits=list(x_waits) + [tok_oc])
        tv1 = b.v(lambda e: e.tensor_scalar_mul(srow_m[:, wsl], st_ps, 1.0 / 1024.0),
                  waits=[st_mm])
        tv2 = b.v(lambda e: e.tensor_scalar_mul(srow_q[:, wsl], sq_ps, 1.0 / 1024.0),
                  waits=[st_mm])
        tv3 = b.v(lambda e: e.tensor_mul(srow_t[:, wsl], srow_m[:, wsl], srow_m[:, wsl]),
                  waits=[tv1])
        tv4 = b.v(lambda e: e.tensor_sub(srow_q[:, wsl], srow_q[:, wsl], srow_t[:, wsl]),
                  waits=[tv2, tv3])
        ts1 = b.s(lambda e: e.activation(srow_t[:, wsl], srow_q[:, wsl], AF.Ln,
                                         bias=epsc[0:1, :]), waits=[tv4, tok_ep])
        ts2 = b.s(lambda e: e.activation(srow_r[:, wsl], srow_t[:, wsl], AF.Exp,
                                         scale=-0.5), waits=[ts1])
        tb1 = b.t(lambda e: e.matmul(mb_ps, onesr, srow_m[:, wsl],
                                     start=True, stop=True),
                  waits=[tv3, tv1, tv2, tok_or])
        te1 = b.v(lambda e: e.tensor_copy(mb[:, wsl], mb_ps), waits=[tb1])
        tb2 = b.t(lambda e: e.matmul(rb_ps, onesr, srow_r[:, wsl],
                                     start=True, stop=True),
                  waits=[ts2, tv2] + ([te1] if same_bank else []))
        te2 = b.v(lambda e: e.tensor_copy(rb[:, wsl], rb_ps), waits=[tb2])
        toks = []
        for c in range(C8):
            sc_buf = scrq[c % 2]
            t1 = b.v(lambda e, c=c, sc_buf=sc_buf: e.tensor_sub(
                sc_buf[:, wsl], x_sb[:, c, sl], mb[:, wsl]), waits=[te1])
            t2 = b.v(lambda e, sc_buf=sc_buf: e.tensor_mul(
                sc_buf[:, wsl], sc_buf[:, wsl], rb[:, wsl]), waits=[te2])
            t3 = b.v(lambda e, c=c, sc_buf=sc_buf: e.tensor_copy(
                xnh_[:, c, wsl], sc_buf[:, wsl]), waits=[t2])
            t4 = b.v(lambda e, c=c, sc_buf=sc_buf: e.scalar_tensor_tensor(
                xnl_[:, c, wsl], xnh_[:, c, wsl], -1.0, sc_buf[:, wsl],
                ALU.mult, ALU.add), waits=[t3])
            ln_state["scr_rd"][c % 2] = t4
            toks.append(t4)
        ln_state["rel"] = [tv1, tv2, te1, te2]
        return toks

    def ffn_ln_cfg(width):
        return (pb[6][0:1, 0:width], pb[7][0:1, 0:width],
                pb[6][:, 0:width], pb[7][:, 0:width], False)

    def attn_ln_cfg(width):
        return (pb[7][0:1, 0:width], pb[7][0:1, 256:256 + width],
                pb[7][:, 0:width], pb[7][:, 256:256 + width], True)

    state = dict(gu_free=[[None], [None]], dn_free=[[None], [None]],
                 store_gu={}, store_d={})

    def emit_ffn(first, x_waits_fn, arena_waits):
        g_free = [None, None]
        u_free = [None, None]
        d_free = [None, None]
        gu_buf = state["gu_free"]
        dn_buf = state["dn_free"]
        h_free = None
        res_all = []
        xn_cur = emit_ln(0, FS, xn[0][0], xn[0][1], ffn_ln_cfg(FS), x_waits_fn(0))
        for s in range(4):
            sl = slice(s * FS, (s + 1) * FS)
            par_s = s % 2
            xnh_, xnl_ = xn[par_s]
            h_toks = []
            for m in range(H24):
                par = m % 2
                if first and s == 0:
                    lg = b.gld(lambda e, m=m, par=par: e.dma_start(
                        out=wgu[par][:, :, 0, :],
                        in_=wgT_h[:, m * 128:(m + 1) * 128].rearrange("(k p) n -> p k n", p=128)),
                        waits=[gu_buf[par]], ch=("gA" if par == 0 else "gB"))
                    lu = b.gld(lambda e, m=m, par=par: e.dma_start(
                        out=wgu[par][:, :, 1, :],
                        in_=wuT_h[:, m * 128:(m + 1) * 128].rearrange("(k p) n -> p k n", p=128)),
                        waits=[gu_buf[par]], ch=("gC" if par == 0 else "gD"))
                    sg = b.ld(lambda e, m=m, par=par: e.dma_start(
                        out=wgu_bf_h[m],
                        in_=wgu[par].rearrange("p k two n -> p (k two n)")),
                        waits=[lg, lu], ch="sA")
                    state["store_gu"][m] = sg
                    wtok = [lg, lu]
                    buf_rel = [sg]
                else:
                    lw = b.ld(lambda e, m=m, par=par: e.dma_start(
                        out=wgu[par].rearrange("p k two n -> p (k two n)"), in_=wgu_bf_h[m]),
                        waits=[gu_buf[par], state["store_gu"].get(m)] +
                              (arena_waits if (s == 0 and m < 2) else []), ch=f"w{par}")
                    wtok = [lw]
                    buf_rel = []
                last_g = None
                gw = wtok + [g_free[par]] + (xn_cur if m == 0 else [])
                for pi, xs in enumerate((xnh_, xnl_)):
                    for k in range(C8):
                        last_g = b.t(lambda e, par=par, k=k, xs=xs, pi=pi, sl=sl: e.matmul(
                            pb[par][:, :], wgu[par][:, k, 0, :], xs[:, k, :],
                            start=(pi == 0 and k == 0), stop=(pi == 1 and k == C8 - 1)),
                            waits=gw if (pi == 0 and k == 0) else [],
                            inc=(pi == 1 and k == C8 - 1))
                last_u = None
                uw = [u_free[par]]
                for pi, xs in enumerate((xnh_, xnl_)):
                    for k in range(C8):
                        last_u = b.t(lambda e, par=par, k=k, xs=xs, pi=pi: e.matmul(
                            pb[2 + par][:, :], wgu[par][:, k, 1, :], xs[:, k, :],
                            start=(pi == 0 and k == 0), stop=(pi == 1 and k == C8 - 1)),
                            waits=uw if (pi == 0 and k == 0) else [],
                            inc=(pi == 1 and k == C8 - 1))
                gu_buf[par] = [last_u] + buf_rel
                tsil = b.s(lambda e, par=par: e.activation(
                    sA[par], pb[par], AF.Silu, scale=S127),
                    waits=[last_g, u_free[par]])
                g_free[par] = tsil
                th1 = b.v(lambda e, par=par: e.tensor_mul(scr_h, sA[par], pb[2 + par]),
                          waits=[tsil, last_u, h_free])
                u_free[par] = th1
                th2 = b.v(lambda e, m=m: e.tensor_copy(hh[:, m, :], scr_h), waits=[th1])
                th3 = b.v(lambda e, m=m: e.scalar_tensor_tensor(
                    hl[:, m, :], hh[:, m, :], -1.0, scr_h, ALU.mult, ALU.add), waits=[th2])
                h_toks.append(th3)
            if s < 3:
                xn_next = emit_ln((s + 1) * FS, FS, xn[1 - par_s][0], xn[1 - par_s][1],
                                  ffn_ln_cfg(FS), x_waits_fn(s + 1))
            sub_res = []
            last_d = None
            for m2 in range(C8):
                par = m2 % 2
                if first and s == 0:
                    ldd = b.gld(lambda e, m2=m2, par=par: e.dma_start(
                        out=wdn[par],
                        in_=wdT_h[:, m2 * 128:(m2 + 1) * 128].rearrange("(k p) n -> p k n", p=128)),
                        waits=[dn_buf[par]], ch=("gE" if par == 0 else "gF"))
                    sd = b.ld(lambda e, m2=m2, par=par: e.dma_start(
                        out=wd_bf_h[m2],
                        in_=wdn[par].rearrange("p k n -> p (k n)")),
                        waits=[ldd], ch="sB")
                    state["store_d"][m2] = sd
                    wtok = [ldd]
                    buf_rel = [sd]
                else:
                    lw = b.ld(lambda e, m2=m2, par=par: e.dma_start(
                        out=wdn[par].rearrange("p k n -> p (k n)"), in_=wd_bf_h[m2]),
                        waits=[dn_buf[par], state["store_d"].get(m2)] +
                              (arena_waits if (s == 0 and m2 < 2) else []), ch=f"d{par}")
                    wtok = [lw]
                    buf_rel = []
                dw = wtok + [d_free[par]] + (h_toks if m2 == 0 else [])
                for pi, hs in enumerate((hh, hl)):
                    for k in range(H24):
                        last_d = b.t(lambda e, par=par, k=k, hs=hs, pi=pi: e.matmul(
                            pb[4 + par][:, :], wdn[par][:, k, :], hs[:, k, :],
                            start=(pi == 0 and k == 0), stop=(pi == 1 and k == H24 - 1)),
                            waits=dw if (pi == 0 and k == 0) else [],
                            inc=(pi == 1 and k == H24 - 1))
                dn_buf[par] = [last_d] + buf_rel
                tres = b.v(lambda e, par=par, m2=m2, sl=sl: e.scalar_tensor_tensor(
                    x_sb[:, m2, sl], pb[4 + par], DOWN_SCALE, x_sb[:, m2, sl],
                    ALU.mult, ALU.add), waits=[last_d])
                d_free[par] = tres
                sub_res.append(tres)
            h_free = last_d
            res_all.extend(sub_res)
            if s < 3:
                xn_cur = xn_next
        return res_all

    attn_state = dict(bank6=None, pt=[None, None], av=[None, None],
                      sbank=[None, None], qmm=[None, None, None, None], pj_mm=None,
                      wq_store={}, wp_store={}, f32_rel=[], pjslot=[None, None])

    def emit_attn_subtile(s8, xn_cur, x_waits_ln, arena_waits):
        asl = slice(s8 * AS, (s8 + 1) * AS)
        wsl = slice(0, AS)
        par_s = s8 % 2
        split = (s8 == 0)
        ev_toks = []
        qb_free = list(attn_state["qmm"])
        ps_free = [None, None]
        f32_free = [None, None]
        nslot = 2 if split else 4
        pend_tlo = {}

        def emit_sw(mm, par):
            sw = b.ld(lambda e, mm=mm, par=par: e.dma_start(
                out=wq_bf_h[mm], in_=wqb[par].rearrange("p k two n -> p (k two n)")),
                waits=[pend_tlo.pop(mm)], ch=f"s{par}")
            attn_state["wq_store"][mm] = sw
            return sw

        for m in range(H24):
            par = m % nslot
            if split:
                lwf = b.ld(lambda e, m=m, par=par: e.dma_start(
                    out=wf32[par],
                    in_=wqkv_h[:, m * 128:(m + 1) * 128].rearrange("(k p) n -> p k n", p=128)),
                    waits=[f32_free[par]] + (arena_waits if m < 4 else []), ch=f"d{par}")
                if m - 2 in pend_tlo:
                    qb_free[par] = [qb_free[par], emit_sw(m - 2, par)]
                thi = b.v(lambda e, par=par: e.tensor_copy(
                    wqb[par][:, :, 0, :], wf32[par]),
                    waits=[lwf, qb_free[par]] + (arena_waits if m < 2 else []))
                tlo = b.v(lambda e, par=par: e.scalar_tensor_tensor(
                    wqb[par][:, :, 1, :], wqb[par][:, :, 0, :], -1.0, wf32[par],
                    ALU.mult, ALU.add), waits=[thi])
                f32_free[par] = tlo
                pend_tlo[m] = tlo
                attn_state.setdefault("f32_rel", [])
                wtok = [tlo]
                buf_rel = None
            else:
                lw = b.ld(lambda e, m=m, par=par: e.dma_start(
                    out=wslots[par].rearrange("p k two n -> p (k two n)"), in_=wq_bf_h[m]),
                    waits=[qb_free[par], attn_state["wq_store"].get(m)] +
                          (attn_state["f32_rel"] if (par >= 2 and m < 4) else []) +
                          (arena_waits if m < 2 else []), ch=f"w{par}")
                wtok = [lw]
                buf_rel = None
            last = None
            mw = wtok + [ps_free[par % 2]] + (xn_cur if m == 0 else []) + \
                ([attn_state["sbank"][par % 2]] if m < 2 else [])
            for pi, (xs, wi) in enumerate([(0, 0), (1, 0), (0, 1)]):
                xbuf = xn[par_s][xs]
                for k in range(C8):
                    last = b.t(lambda e, par=par, k=k, xbuf=xbuf, wi=wi, pi=pi: e.matmul(
                        pb[par % 2][:, 0:AS], wslots[par][:, k, wi, :], xbuf[:, k, wsl],
                        start=(pi == 0 and k == 0), stop=(pi == 2 and k == C8 - 1)),
                        waits=mw if (pi == 0 and k == 0) else [],
                        inc=(pi == 2 and k == C8 - 1))
            qb_free[par] = last if buf_rel is None else [last, buf_rel]
            tev = b.s(lambda e, m=m, par=par: e.activation(
                qkv_sb[:, m, :], pb[par % 2][:, 0:AS], AF.Identity,
                bias=bqs[:, m:m + 1], scale=(0.125 if m < C8 else 1.0)),
                waits=[last, tok_bqs])
            ps_free[par % 2] = tev
            ev_toks.append(tev)
        if split:
            for mm in (H24 - 2, H24 - 1):
                p_ = mm % 2
                tl = pend_tlo[mm]
                sw = emit_sw(mm, p_)
                qb_free[p_] = [qb_free[p_], sw]
                attn_state["f32_rel"] = attn_state["f32_rel"][-4:] + [tl, sw]
        attn_state["qmm"] = list(qb_free) + ([None, None] if split else [])
        if s8 < (T // AS) - 1:
            xn_next = emit_ln((s8 + 1) * AS, AS, xn[1 - par_s][0], xn[1 - par_s][1],
                              attn_ln_cfg(AS), x_waits_ln)
        else:
            xn_next = None
        b6 = attn_state["bank6"]
        g_toks = []
        sbanks = [pb[0], pb[1], pb[7]]
        relcur = list(ln_state["rel"])
        s_free = [ev_toks[22], ev_toks[23], relcur]
        e4_free = [[], [], []]
        E4x = E4 + [sA[0]]
        pt_free = attn_state["pt"]
        av_free = attn_state["av"]
        vt_ev = [[None, None], [None, None]]
        for g in range(2):
            gsl = slice(g * 128, (g + 1) * 128)
            for r in range(2):
                lastt = None
                for j in range(4):
                    jc = r * 4 + j
                    lastt = b.t(lambda e, jc=jc, j=j, gsl=gsl: e.transpose(
                        pb[6][:, j * 128:(j + 1) * 128], qkv_sb[:, 16 + jc, gsl], identF),
                        waits=[ev_toks[16 + r * 4 + 3], tok_id, b6] if j == 0 else [])
                tv = b.v(lambda e, r=r, g=g: e.tensor_copy(
                    VT[:, g, r * 512:(r + 1) * 512], pb[6][:, 0:512]), waits=[lastt])
                b6 = tv
                vt_ev[g][r] = tv
        tex_last = {}
        tpe_tok = {}
        texp_tok = {}
        av_g_last = {0: [], 1: []}

        def wave_heads(w):
            return [8 * (w // 2) + (w % 2) + 2 * j for j in range(4)]

        def emit_qk(i):
            g, w = i // 4, i % 4
            p3 = i % 3
            bnk = sbanks[p3]
            gsl = slice(g * 128, (g + 1) * 128)
            heads = wave_heads(w)
            off = 64 * (w % 2)
            last = None
            for j in range(4):
                h = heads[j]
                jcq = h // 2
                last = b.t(lambda e, j=j, jcq=jcq, off=off, gsl=gsl, bnk=bnk: e.matmul(
                    bnk[:, j * 128:(j + 1) * 128],
                    qkv_sb[off:off + 64, jcq, gsl],
                    qkv_sb[off:off + 64, 8 + jcq, gsl], start=True, stop=True),
                    waits=[ev_toks[23], s_free[p3]] if j == 0 else [])
            bsl = B_sb.rearrange("p (a two) b -> p two a b", two=2)[
                :, w % 2, 4 * (w // 2):4 * (w // 2) + 4, :]
            tba = b.v(lambda e, bsl=bsl, bnk=bnk: e.tensor_add(
                bnk[:, 0:512], bnk[:, 0:512], bsl),
                waits=[last, tok_B])
            tmx = b.v(lambda e, p3=p3, bnk=bnk: e.reduce_max(
                nm4[p3], bnk[:, 0:512].rearrange("p (a b) -> p a b", b=128),
                axis=AX, negate=True), waits=[tba])
            texl = None
            for j in range(4):
                texl = b.s(lambda e, j=j, p3=p3, bnk=bnk: e.activation(
                    E4x[p3][:, j * 128:(j + 1) * 128],
                    bnk[:, j * 128:(j + 1) * 128], AF.Exp,
                    bias=nm4[p3][:, j:j + 1],
                    accum_out=sm4[p3][:, j:j + 1]),
                    waits=([tmx] + e4_free[p3]) if j == 0 else [])
            s_free[p3] = texl
            tex_last[i] = texl
            rsl = rg.rearrange("p g (a two) -> p g two a", two=2)[
                :, g, w % 2, 4 * (w // 2):4 * (w // 2) + 4]
            tln = b.s(lambda e, rsl=rsl, p3=p3: e.activation(
                rsl, sm4[p3], AF.Ln), waits=[texl])
            texp = b.s(lambda e, rsl=rsl: e.activation(
                rsl, rsl, AF.Exp, scale=-1.0), waits=[tln])
            texp_tok[i] = texp
            e4_free[p3] = []

        def emit_pt(i):
            par = i % 2
            p3 = i % 3
            lastt = None
            for j in range(4):
                lastt = b.t(lambda e, j=j, par=par, p3=p3: e.transpose(
                    pb[2 + par][:, j * 128:(j + 1) * 128],
                    E4x[p3][:, j * 128:(j + 1) * 128], identF),
                    waits=[tex_last[i], pt_free[par], tok_id] if j == 0 else [])
            tpe = b.s(lambda e, par=par: e.activation(
                PT4[par], pb[2 + par][:, 0:512], AF.Identity), waits=[lastt])
            pt_free[par] = tpe
            tpe_tok[i] = tpe
            e4_free[p3].append(lastt)

        def emit_av(i):
            g, w, par = i // 4, i % 4, i % 2
            heads = wave_heads(w)
            lasta = None
            for j in range(4):
                h = heads[j]
                lasta = b.t(lambda e, j=j, h=h, g=g, par=par: e.matmul(
                    pb[4 + par][:, j * 64:(j + 1) * 64],
                    PT4[par][:, j * 128:(j + 1) * 128],
                    VT[:, g, h * 64:(h + 1) * 64], start=True, stop=True),
                    waits=[tpe_tok[i], vt_ev[g][w // 2], av_free[par]] if j == 0 else [])
            tosl = None
            for j in range(4):
                h = heads[j]
                tosl = b.v(lambda e, j=j, h=h, g=g, par=par: e.tensor_scalar_mul(
                    O_sb[:, g, h * 64:(h + 1) * 64], pb[4 + par][:, j * 64:(j + 1) * 64],
                    rg[:, g, h:h + 1]),
                    waits=[lasta, texp_tok[i]] if j == 0 else [])
            av_free[par] = tosl
            av_g_last[g] = (av_g_last[g] + [tosl])[-2:]

        ot_rd = [[], []]

        def emit_ot(g):
            nonlocal b6
            gsl = slice(g * 128, (g + 1) * 128)
            for r in range(2):
                lastt = None
                for j in range(4):
                    jc = r * 4 + j
                    lastt = b.t(lambda e, jc=jc, j=j, g=g: e.transpose(
                        pb[6][:, j * 128:(j + 1) * 128],
                        O_sb[:, g, jc * 128:(jc + 1) * 128], identF),
                        waits=av_g_last[g] + [b6] if j == 0 else [])
                ot_rd[g].append(lastt)
                t1 = b.v(lambda e, r=r, gsl=gsl: e.tensor_copy(
                    oTh[:, r * 4:(r + 1) * 4, gsl],
                    pb[6][:, 0:512].rearrange("p (a b) -> p a b", b=128)), waits=[lastt])
                t2 = b.v(lambda e, r=r, gsl=gsl: e.scalar_tensor_tensor(
                    oTl[:, r * 4:(r + 1) * 4, gsl], oTh[:, r * 4:(r + 1) * 4, gsl], -1.0,
                    pb[6][:, 0:512].rearrange("p (a b) -> p a b", b=128),
                    ALU.mult, ALU.add), waits=[t1])
                b6 = t2
                g_toks.append(t2)

        for i in range(10):
            if i < 8:
                emit_qk(i)
            if 1 <= i <= 8:
                emit_pt(i - 1)
            if 2 <= i <= 9:
                emit_av(i - 2)
            if i - 2 == 3:
                emit_ot(0)
        emit_ot(1)
        attn_state["sbank"] = [s_free[0], s_free[1]]
        ln_state["rel"] = list(ln_state["rel"]) + [tex_last[5]]
        attn_state["pt"] = pt_free
        attn_state["av"] = av_free
        res_toks = []
        pj_free = [None, None]
        wb_free = list(qb_free) + ([None, None] if split else [])
        f32_free = [None, None]
        last_mm = attn_state["pj_mm"]
        pw_free = list(attn_state["pjslot"])
        pend_ptlo = {}

        def emit_psw(mm, par):
            sw = b.ld(lambda e, mm=mm, par=par: e.dma_start(
                out=wp_bf_h[mm], in_=wpb[par].rearrange("p k two n -> p (k two n)")),
                waits=[pend_ptlo.pop(mm)], ch=f"s{par}")
            attn_state["wp_store"][mm] = sw
            return sw

        for m2 in range(C8):
            par = m2 % nslot
            if split:
                lwf = b.ld(lambda e, m2=m2, par=par: e.dma_start(
                    out=wf32[par],
                    in_=wproj_h[:, m2 * 128:(m2 + 1) * 128].rearrange("(k p) n -> p k n", p=128)),
                    waits=[f32_free[par]] + ([ev_toks[-1]] if m2 < 2 else []), ch=f"d{par}")
                if m2 - 2 in pend_ptlo:
                    wb_free[par] = [wb_free[par], emit_psw(m2 - 2, par)]
                thi = b.v(lambda e, par=par: e.tensor_copy(
                    wpb[par][:, :, 0, :], wf32[par]),
                    waits=[lwf, wb_free[par]] + ([attn_state["qmm"][par]] if m2 < 2 else []))
                tlo = b.v(lambda e, par=par: e.scalar_tensor_tensor(
                    wpb[par][:, :, 1, :], wpb[par][:, :, 0, :], -1.0, wf32[par],
                    ALU.mult, ALU.add), waits=[thi])
                f32_free[par] = tlo
                pend_ptlo[m2] = tlo
                wtok = [tlo]
                buf_rel = None
            else:
                lw = b.ld(lambda e, m2=m2, par=par: e.dma_start(
                    out=wslots[par].rearrange("p k two n -> p (k two n)"), in_=wp_bf_h[m2]),
                    waits=[wb_free[par], attn_state["wp_store"].get(m2)], ch=f"w{par}")
                wtok = [lw]
                buf_rel = None
            bnk = pb[6] if m2 % 2 == 0 else pb[2]
            pjw = pj_free[m2 % 2]
            if m2 < 2:
                pjw = [pj_free[m2 % 2], b6] + (g_toks if m2 == 0 else []) + \
                    ([pt_free[0]] if m2 == 1 else [])
            last = None
            mw = wtok + [pjw]
            for pi, (xs, wi) in enumerate([(oTh, 0), (oTl, 0), (oTh, 1)]):
                for k in range(C8):
                    last = b.t(lambda e, par=par, k=k, xs=xs, wi=wi, pi=pi, bnk=bnk: e.matmul(
                        bnk[:, 0:AS], wslots[par][:, k, wi, :], xs[:, k, :],
                        start=(pi == 0 and k == 0), stop=(pi == 2 and k == C8 - 1)),
                        waits=mw if (pi == 0 and k == 0) else [],
                        inc=(pi == 2 and k == C8 - 1))
            wb_free[par] = last if buf_rel is None else [last, buf_rel]
            last_mm = last
            a_half = a_sb[:, 0:AS] if False else a_sb
            tev = b.s(lambda e, m2=m2, bnk=bnk: e.activation(
                asb2[m2 % 2], bnk[:, 0:AS], AF.Identity, bias=bps[:, m2:m2 + 1]),
                waits=[last, tok_bp])
            tres = b.v(lambda e, m2=m2, asl=asl: e.tensor_add(
                x_sb[:, m2, asl], x_sb[:, m2, asl], asb2[m2 % 2]), waits=[tev])
            pj_free[m2 % 2] = tres
            res_toks.append(tres)
        attn_state["pj_mm"] = last_mm
        if split:
            for mm in (C8 - 2, C8 - 1):
                p_ = mm % 2
                tl = pend_ptlo[mm]
                sw = emit_psw(mm, p_)
                wb_free[p_] = [wb_free[p_], sw]
                attn_state["f32_rel"] = attn_state["f32_rel"][-4:] + [tl, sw]
        attn_state["qmm"] = list(wb_free)[:4]
        attn_state["bank6"] = pj_free[0]
        attn_state["pt"][0] = pj_free[1]
        return res_toks, xn_next

    r1 = emit_ffn(first=True,
                  x_waits_fn=lambda s: [tok_x[s]],
                  arena_waits=[])
    cur = r1
    if stage >= 2:
        allres = []
        arena_w = [state["gu_free"][0], state["gu_free"][1],
                   state["dn_free"][0], state["dn_free"][1]]
        xn_cur = emit_ln(0, AS, xn[0][0], xn[0][1], attn_ln_cfg(AS), r1)
        for s8 in range(T // AS):
            rs, xn_cur = emit_attn_subtile(s8, xn_cur, r1,
                                           arena_w if s8 == 0 else [])
            allres.extend(rs)
        cur = allres
    if stage >= 3:
        arena_w2 = list(attn_state["qmm"]) + [attn_state["pj_mm"], attn_state["bank6"]]
        r3 = emit_ffn(first=False,
                      x_waits_fn=lambda s: allres[2 * s * C8:(2 * s + 2) * C8],
                      arena_waits=arena_w2)
        cur = r3

    tok_y = b.ld(lambda e: e.dma_start(
        out=yT_h.rearrange("(c p) t -> p c t", p=128), in_=x_sb), waits=cur, ch="y")
    b.emit("sync", lambda e: e.nop(), waits=[tok_y])

    b.replay(nc)
    return nc


def _bias_tiles(bias_table):
    bt = np.asarray(bias_table, dtype=np.float32)
    q = np.arange(8)
    k = np.arange(8)
    idx = 112 + q[:, None] - k[None, :]
    bias8 = bt[idx.reshape(-1)].reshape(8, 8, HD)
    B_full = np.full((128, HD, 128), NEG, dtype=np.float32)
    for w in range(16):
        B_full[w * 8:(w + 1) * 8, :, w * 8:(w + 1) * 8] = bias8.transpose(0, 2, 1)
    return B_full


def kernel(L0_attn_x, L0_attn_w_qkv, L0_attn_b_qkv, L0_attn_w_proj, L0_attn_b_proj,
           L0_attn_bias_table, L0_ffn_w_gate, L0_ffn_w_up, L0_ffn_w_down):
    x = np.asarray(L0_attn_x)
    xr = np.roll(x, -SHIFT, axis=1).reshape(B_ * N_, C_)
    shards = [np.ascontiguousarray(xr[c * T:(c + 1) * T].T) for c in range(NCORES)]

    wqkv = np.ascontiguousarray(np.asarray(L0_attn_w_qkv, dtype=np.float32))
    wproj = np.ascontiguousarray(np.asarray(L0_attn_w_proj, dtype=np.float32))
    wgT = np.ascontiguousarray(np.asarray(L0_ffn_w_gate).T)
    wuT = np.ascontiguousarray(np.asarray(L0_ffn_w_up).T)
    wdT = np.ascontiguousarray(np.asarray(L0_ffn_w_down).T)
    bq = np.ascontiguousarray(np.asarray(L0_attn_b_qkv, dtype=np.float32).reshape(H24, 128).T)
    bp = np.ascontiguousarray(np.asarray(L0_attn_b_proj, dtype=np.float32).reshape(C8, 128).T)
    Bt = _bias_tiles(L0_attn_bias_table)
    ident = np.eye(128, dtype=np.float32)
    onesc = np.ones((128, 1), dtype=np.float32)
    epsc = np.full((128, 1), EPS, dtype=np.float32)
    onesr = np.ones((1, 128), dtype=np.float32)

    nc = build_nc(STAGE)
    common = dict(wqkv=wqkv, wproj=wproj, wgT=wgT, wuT=wuT, wdT=wdT,
                  bq=bq, bp=bp, biasm=Bt, ident=ident, onesc=onesc, onesr=onesr,
                  epsc=epsc)
    in_maps = [dict(common, xT=shards[c]) for c in range(NCORES)]
    res = run_bass_kernel_spmd(nc, in_maps, core_ids=list(range(NCORES)))
    out = np.concatenate([np.asarray(res.results[c]["yT"]).T for c in range(NCORES)], axis=0)
    return np.roll(out.reshape(B_, N_, C_), SHIFT, axis=1)
